# version 20
# baseline (speedup 1.0000x reference)
import numpy as np
import ml_dtypes
from contextlib import ExitStack

import concourse.bass as bass
import concourse.mybir as mybir
from concourse.bass_utils import run_bass_kernel_spmd

F32 = mybir.dt.float32
BF16 = mybir.dt.bfloat16
I8 = mybir.dt.int8
AF = mybir.ActivationFunctionType
ALU = mybir.AluOpType
AX = mybir.AxisListType

D = 2048
NCH = 16
SC = 4096
T = 512
NT = SC // T
NEC = 48
NOC = 64
NG = SC // 128
EPS = 1e-5

ENGINES = ("tensor", "vector", "scalar", "sync", "gpsimd")


class Sem:
    __slots__ = ("h", "n", "name")
    def __init__(self, h, name):
        self.h = h
        self.n = 0
        self.name = name


class Prog:
    def __init__(self):
        self.ops = {e: [] for e in ENGINES}
        self.lastwait = {e: {} for e in ENGINES}
        self.pending = []

    def marker(self, *sems):
        self.pending += [(s, s.n) for s in sems]

    def emit(self, eng, fn, waits=(), inc=None, incv=1, no_pending=False):
        ws = []
        for (sem, val) in (list(waits) if no_pending else list(self.pending) + list(waits)):
            if val <= 0:
                continue
            lw = self.lastwait[eng].get(id(sem), 0)
            if val > lw:
                ws.append((sem, val))
                self.lastwait[eng][id(sem)] = val
        if inc is not None:
            inc.n += incv
        self.ops[eng].append((fn, ws, inc, incv))
        return inc.n if inc is not None else None

    def replay(self, eng, name):
        for fn, ws, inc, incv in self.ops[name]:
            for sem, val in ws:
                eng.wait_ge(sem.h, val)
            if fn is None:
                continue
            ins = fn(eng)
            if inc is not None:
                ins.then_inc(inc.h, incv)


def make_identity(nc, identity):
    sq = identity.shape[0]
    nc.gpsimd.memset(identity, 0.0)
    return nc.gpsimd.affine_select(
        out=identity, in_=identity, compare_op=ALU.not_equal,
        fill=1.0, base=0, pattern=[[-1, sq]], channel_multiplier=1)


def build_program(phases="ABCEDF", dump=()):
    nc = bass.Bass("TRN2", target_bir_lowering=False)
    p = Prog()

    def dram(name, shape, dtype, kind="Internal"):
        if name in dump:
            kind = "ExternalOutput"
        return nc.dram_tensor(name, shape, dtype, kind=kind).ap()

    x = nc.dram_tensor("x", [SC, D], F32, kind="ExternalInput").ap()
    win = nc.dram_tensor("win", [NEC, 128, NCH, 128], F32, kind="ExternalInput").ap()
    b_in = nc.dram_tensor("b_in", [128, NEC], F32, kind="ExternalInput").ap()
    wout = nc.dram_tensor("wout", [NCH, 128, NCH, 128], F32, kind="ExternalInput").ap()
    b_out = nc.dram_tensor("b_out", [128, NCH], F32, kind="ExternalInput").ap()
    wg = nc.dram_tensor("wg", [NOC, 128, NCH, 128], I8, kind="ExternalInput").ap()
    wu = nc.dram_tensor("wu", [NOC, 128, NCH, 128], I8, kind="ExternalInput").ap()
    wd = nc.dram_tensor("wd", [NCH, 128, NOC, 128], I8, kind="ExternalInput").ap()
    sg = nc.dram_tensor("sg", [128, NOC], F32, kind="ExternalInput").ap()
    su = nc.dram_tensor("su", [128, NOC], F32, kind="ExternalInput").ap()
    sd = nc.dram_tensor("sd", [128, NCH], F32, kind="ExternalInput").ap()
    cosb = nc.dram_tensor("cosb", [4, 128, SC], F32, kind="ExternalInput").ap()
    sinb = nc.dram_tensor("sinb", [4, 128, SC], F32, kind="ExternalInput").ap()
    out = nc.dram_tensor("out", [SC, D], F32, kind="ExternalOutput").ap()

    xn_s = dram("xn_s", [NCH, 128, SC], BF16)
    q_s = dram("q_s", [NCH, 128, SC], BF16)
    k_s = dram("k_s", [NCH, 128, SC], BF16)
    v_s = dram("v_s", [NCH, 128, SC], BF16)
    fuse_s = dram("fuse_s", [NT, 128, NOC * T], BF16)
    yo_s = dram("yo_s", [NCH, 128, SC], BF16)
    cc_a = nc.dram_tensor("cc_a", [NCH, 128], F32)
    cc_b = nc.dram_tensor("cc_b", [NCH, 128], F32)
    a_dbg = dram("a_dbg", [128, NCH], F32) if "a_dbg" in dump else None

    gst = ExitStack()
    sb = lambda name, shape, dt: gst.enter_context(nc.sbuf_tensor(name, shape, dt))
    ps = lambda name, shape, dt: gst.enter_context(nc.psum_tensor(name, shape, dt))
    mksem = lambda name: Sem(gst.enter_context(nc.semaphore(name)), name)

    id_bf = sb("id_bf", [128, 128], BF16)
    id_f32 = sb("id_f32", [128, 128], F32)
    ones_bf = sb("ones_bf", [128, 1], BF16)
    ones_f32 = sb("ones_f32", [1, 128], F32)
    b_in_t = sb("b_in_t", [128, NEC], F32)
    bo_t = sb("bo_t", [128, NCH], F32)
    sg_t = sb("sg_t", [128, NOC], F32)
    su_t = sb("su_t", [128, NOC], F32)
    sd_t = sb("sd_t", [128, NCH], F32)
    A_full = sb("A_full", [128, NCH], F32)
    eps_t = sb("eps_t", [128, 1], F32)
    neghalf_t = sb("neghalf_t", [128, 1], F32)
    A_acc = sb("A_acc", [128, NCH], F32)
    att = sb("att", [NCH, 128], F32)
    af_t = sb("af_t", [NCH, 128], F32)

    all_sems = []
    def S(name):
        s = mksem(name)
        all_sems.append(s)
        return s

    def SA(name, n):
        return [S(f"{name}{i}") for i in range(n)]

    g_const = S("g_const")
    s_const = S("s_const")

    for idt in (id_bf, id_f32):
        p.emit("gpsimd", lambda g, idt=idt: g.memset(idt[:, :], 0.0), inc=g_const)
        p.emit("gpsimd", lambda g, idt=idt: g.affine_select(
                   out=idt[:, :], in_=idt[:, :], compare_op=ALU.not_equal,
                   fill=1.0, base=0, pattern=[[-1, 128]], channel_multiplier=1),
               waits=[(g_const, g_const.n)], inc=g_const)
    p.emit("gpsimd", lambda g: g.memset(ones_bf[:], 1.0), inc=g_const)
    p.emit("gpsimd", lambda g: g.memset(ones_f32[:], 1.0), inc=g_const)
    p.emit("gpsimd", lambda g: g.memset(eps_t[:], EPS), inc=g_const)
    p.emit("gpsimd", lambda g: g.memset(neghalf_t[:], -0.5), inc=g_const)
    for dst, src in ((b_in_t, b_in), (bo_t, b_out), (sg_t, sg), (su_t, su), (sd_t, sd)):
        p.emit("sync", lambda s, dst=dst, src=src: s.dma_start(out=dst[:], in_=src),
               inc=s_const, incv=16)
    p.marker(g_const, s_const)

    big = sb("big", [128, NCH * SC], BF16)
    xn_all = big
    y_all = big
    fuse_ts = big

    a_xld = SA("a_xld", 3); a_bns = S("a_bns"); a_mv = S("a_mv"); a_sq = S("a_sq"); a_ln = S("a_ln")
    a_xn = S("a_xn")
    a_tr = S("a_tr"); a_evS = S("a_evS"); a_evV = S("a_evV"); a_st = S("a_st")


    b_wld = SA("b_wld", 3); b_mm = S("b_mm"); b_evac = S("b_evac"); b_rp = S("b_rp")
    b_csld = SA("b_csld", 3); b_st = SA("b_st", 4); b_rt = S("b_rt")
    ab_st = ExitStack()
    bgen = None
    if "B" in phases:
        bst = ab_st
        bsb = lambda n, sh, dt: bst.enter_context(nc.sbuf_tensor(n, sh, dt))
        wbuf = [bsb(f"wbufB{i}", [128, NCH * 128], BF16) for i in range(3)]
        lead_buf = [bsb(f"lead_buf{i}", [128, SC], BF16) for i in range(2)]
        trail_tmp = [bsb(f"trail{i}", [128, T], BF16) for i in range(4)]
        cs_cos = [bsb(f"cscos{i}", [128, T], BF16) for i in range(3)]
        cs_sin = [bsb(f"cssin{i}", [128, T], BF16) for i in range(3)]
        rtmp1 = bsb("rtmp1", [128, T], BF16)
        rtmp2 = bsb("rtmp2", [128, T], BF16)
        stg = [bsb(f"stgB{i}", [128, T], BF16) for i in range(4)]
        psB = [bst.enter_context(nc.psum_tensor(f"psB{i}", [128, T], F32)) for i in range(4)]

        eorder = [0, 4, 1, 5, 2, 6, 3, 7] + list(range(8, 16)) + \
                 [16, 20, 17, 21, 18, 22, 19, 23] + list(range(24, 32)) + list(range(32, 48))
        def chunk_target(ec):
            if ec < 16:
                return q_s, ec
            if ec < 32:
                return k_s, ec - 16
            return v_s, ec - 32

        st_count = 0
        bst_uses = [0, 0, 0, 0]
        cs_count = 0
        pairs_done = 0

        def gen_B():
            nonlocal st_count, cs_count, pairs_done
            for idx, ec in enumerate(eorder):
                is_lead = ec in (0, 1, 2, 3, 16, 17, 18, 19)
                is_trail = ec in (4, 5, 6, 7, 20, 21, 22, 23)
                jc = (ec % 16) % 4
                p.emit("gpsimd", lambda g, ec=ec, idx=idx: g.dma_start(
                           out=wbuf[idx % 3][:], in_=win[ec]),
                       waits=[(b_mm, 8 * (idx - 2)), (g_const, 8)],
                       inc=b_wld[idx % 3], incv=16, no_pending=(idx < 3))
                tgt, tc = chunk_target(ec)
                for ts in range(NT):
                    m = idx * NT + ts
                    bank = m % 4
                    for c in range(NCH):
                        p.emit("tensor", lambda t, idx=idx, bank=bank, c=c, ts=ts: t.matmul(
                                   psB[bank][:],
                                   wbuf[idx % 3][:, c*128:(c+1)*128],
                                   xn_all[:, c*SC + ts*T: c*SC + (ts+1)*T],
                                   start=(c == 0), stop=(c == NCH - 1)),
                               waits=[(b_wld[idx % 3], 16 * (idx // 3 + 1)), (b_evac, m - 3),
                                      (a_evS, (4 * (ts + 1)) if "A" in phases else 0)],
                               inc=(b_mm if c == NCH - 1 else None))
                    if is_lead:
                        p.emit("scalar", lambda sc, bank=bank, ec=ec, ts=ts, lb=pairs_done % 2: sc.activation(
                                   lead_buf[lb][:, ts*T:(ts+1)*T], psB[bank][:], AF.Identity,
                                   bias=b_in_t[:, ec:ec+1]),
                               waits=[(b_mm, m + 1), (b_rp, 16 * (pairs_done - 1))],
                               inc=b_evac)
                    elif is_trail:
                        csl = cs_count % 3
                        p.emit("gpsimd", lambda g, jc=jc, ts=ts, csl=csl: g.dma_start(
                                   out=cs_cos[csl][:], in_=cosb[jc][:, ts*T:(ts+1)*T]),
                               waits=[(b_rp, 2 * (cs_count - 2))], inc=b_csld[csl], incv=16)
                        p.emit("gpsimd", lambda g, jc=jc, ts=ts, csl=csl: g.dma_start(
                                   out=cs_sin[csl][:], in_=sinb[jc][:, ts*T:(ts+1)*T]),
                               inc=b_csld[csl], incv=16)
                        cs_count += 1
                        p.emit("scalar", lambda sc, bank=bank, ec=ec, ts=ts: sc.activation(
                                   trail_tmp[ts % 4][:], psB[bank][:], AF.Identity,
                                   bias=b_in_t[:, ec:ec+1]),
                               waits=[(b_mm, m + 1), (b_rp, 2 * cs_count - 8)],
                               inc=b_evac)
                        slA = st_count % 4
                        slB = (st_count + 1) % 4
                        N0 = cs_count - 1
                        lb = pairs_done % 2
                        p.emit("vector", lambda v, ts=ts, csl=csl, lb=lb: v.tensor_mul(
                                   rtmp1[:], lead_buf[lb][:, ts*T:(ts+1)*T], cs_cos[csl][:]),
                               waits=[(b_evac, b_evac.n), (b_csld[csl], 32 * ((cs_count - 1) // 3 + 1)),
                                      (b_rp, 2 * N0),
                                      (b_st[st_count % 4], 16 * bst_uses[st_count % 4]),
                                      (b_st[(st_count + 1) % 4], 16 * bst_uses[(st_count + 1) % 4])],
                               inc=b_rt)
                        p.emit("vector", lambda v, ts=ts, csl=csl: v.tensor_mul(
                                   rtmp2[:], trail_tmp[ts % 4][:], cs_sin[csl][:]), inc=b_rt)
                        p.emit("vector", lambda v, slA=slA: v.tensor_sub(stg[slA][:], rtmp1[:], rtmp2[:]),
                               waits=[(b_rt, 4 * N0 + 2)], inc=b_rp)
                        p.emit("vector", lambda v, ts=ts, csl=csl: v.tensor_mul(
                                   rtmp1[:], trail_tmp[ts % 4][:], cs_cos[csl][:]),
                               waits=[(b_rp, 2 * N0 + 1)], inc=b_rt)
                        p.emit("vector", lambda v, ts=ts, csl=csl, lb=lb: v.tensor_mul(
                                   rtmp2[:], lead_buf[lb][:, ts*T:(ts+1)*T], cs_sin[csl][:]), inc=b_rt)
                        p.emit("vector", lambda v, slB=slB: v.tensor_add(stg[slB][:], rtmp1[:], rtmp2[:]),
                               waits=[(b_rt, 4 * N0 + 4)], inc=b_rp)
                        ltg, ltc = chunk_target(ec - 4)
                        p.emit("sync", lambda s, ltg=ltg, ltc=ltc, ts=ts, slA=slA: s.dma_start(
                                   out=ltg[ltc][:, ts*T:(ts+1)*T], in_=stg[slA][:]),
                               waits=[(b_rp, b_rp.n)], inc=b_st[slA], incv=16)
                        p.emit("sync", lambda s, tgt=tgt, tc=tc, ts=ts, slB=slB: s.dma_start(
                                   out=tgt[tc][:, ts*T:(ts+1)*T], in_=stg[slB][:]),
                               inc=b_st[slB], incv=16)
                        bst_uses[slA] += 1; bst_uses[slB] += 1
                        st_count += 2
                    else:
                        sl = st_count % 4
                        p.emit("scalar", lambda sc, bank=bank, ec=ec, sl=sl: sc.activation(
                                   stg[sl][:], psB[bank][:], AF.Identity,
                                   bias=b_in_t[:, ec:ec+1]),
                               waits=[(b_mm, m + 1), (b_st[sl], 16 * bst_uses[sl])],
                               inc=b_evac)
                        p.emit("sync", lambda s, tgt=tgt, tc=tc, ts=ts, sl=sl: s.dma_start(
                                   out=tgt[tc][:, ts*T:(ts+1)*T], in_=stg[sl][:]),
                               waits=[(b_evac, b_evac.n)], inc=b_st[sl], incv=16)
                        bst_uses[sl] += 1
                        st_count += 1
                    yield (idx, ts)
                if is_trail:
                    pairs_done += 1
        bgen = gen_B()
    b_next = [0]

    def pump_B(k):
        if bgen is None:
            return
        limit = (6 * (k + 1)) if k >= 0 else 10 ** 9
        while b_next[0] < min(limit, len(eorder) * NT if bgen else 0):
            if k >= 0 and b_next[0] % NT > k:
                break
            if next(bgen, None) is None:
                break
            b_next[0] += 1

    if "A" in phases:
        ast = ab_st
        asb_ = lambda n, sh, dt: ast.enter_context(nc.sbuf_tensor(n, sh, dt))
        xg = [asb_(f"xg{i}", [128, D], F32) for i in range(3)]
        xn_tm = [asb_(f"xn_tm{i}", [128, D], BF16) for i in range(2)]
        bn6 = [asb_(f"bn6{i}", [128, 24], F32) for i in range(3)]
        mv = [asb_(f"mv{i}", [128, 2], F32) for i in range(3)]
        sstd = [asb_(f"sstd{i}", [128, 1], F32) for i in range(3)]
        rstd = [asb_(f"rstd{i}", [128, 1], F32) for i in range(3)]
        psA = [ast.enter_context(nc.psum_tensor(f"psA{i}", [128, D], BF16)) for i in range(2)]

        def emit_xld(g):
            p.emit("scalar", lambda s, g=g: s.dma_start(out=xg[g % 3][:], in_=x[g*128:(g+1)*128, :]),
                   waits=[(a_xn, g - 2)], inc=a_xld[g % 3], incv=16, no_pending=True)

        for g in range(NG):
            b = g % 3
            if g == 0:
                for gg in range(3):
                    emit_xld(gg)
            p.emit("vector", lambda v, b=b: v.bn_stats(bn6[b][:, 0:6], xg[b][:, 0:512]),
                   waits=[(a_xld[b], 16 * (g // 3 + 1)), (a_mv, g - 2)])
            for i in range(1, 4):
                p.emit("vector", lambda v, b=b, i=i: v.bn_stats(bn6[b][:, 6*i:6*i+6], xg[b][:, 512*i:512*(i+1)]),
                       inc=(a_bns if i == 3 else None))
            p.emit("vector", lambda v, b=b: v.bn_aggr(mv[b][:], bn6[b][:]),
                   waits=[(a_bns, g + 1), (a_ln, g - 2)], inc=a_mv)
            p.emit("scalar", lambda sc, b=b: sc.activation(sstd[b][:], mv[b][:, 1:2], AF.Ln, bias=eps_t[:]),
                   waits=[(a_mv, g + 1)], inc=a_ln)
            p.emit("scalar", lambda sc, b=b: sc.activation(rstd[b][:], sstd[b][:], AF.Exp, scale=neghalf_t[:]),
                   waits=[(a_xn, g - 2)], inc=a_sq)
            p.emit("vector", lambda v, b=b, b3=g % 2: v.tensor_scalar(
                       xn_tm[b3][:], xg[b][:], mv[b][:, 0:1], rstd[b][:], ALU.subtract, ALU.mult),
                   waits=[(a_sq, g + 1), (a_tr, g - 1), (a_xld[b], 16 * (g // 3 + 1))],
                   inc=a_xn)
            if g + 3 < NG:
                emit_xld(g + 3)
            for c in range(NCH):
                p.emit("tensor", lambda t, b2=g % 2, b3=g % 2, c=c: t.transpose(
                           psA[b2][:, c*128:(c+1)*128], xn_tm[b3][:, c*128:(c+1)*128], id_bf[:]),
                       waits=[(a_xn, g + 1), (g_const, 1), (a_evS, g - 1)],
                       inc=(a_tr if c == NCH - 1 else None))
            for c in range(NCH):
                p.emit("scalar", lambda sc, b2=g % 2, c=c, g=g: sc.copy(
                           xn_all[:, c*SC + g*128: c*SC + (g+1)*128], psA[b2][:, c*128:(c+1)*128]),
                       waits=[(a_tr, g + 1)], inc=(a_evS if c == NCH - 1 else None))
        p.marker(a_evS, a_tr, a_xn, *a_xld)
    if bgen is not None:
        pump_B(-1)
        p.marker(b_mm, b_evac, b_rp, *b_st, *b_csld, *b_wld)

    ab_st.close()

    c_kld = S("c_kld"); c_vld = SA("c_vld", 2); c_ksq = S("c_ksq"); c_nmm = S("c_nmm")
    c_nrm = S("c_nrm"); c_mx = S("c_mx"); c_inv = S("c_inv"); c_bmm = S("c_bmm"); c_invb = S("c_invb"); c_pc = S("c_pc")
    c_dve = S("c_dve"); c_red = S("c_red"); c_fin = S("c_fin"); c_cc = S("c_cc")
    c_af1 = S("c_af1"); c_af2 = S("c_af2"); c_af3 = S("c_af3"); c_af = S("c_af")
    e_wld = SA("e_wld", 3); e_gmm = S("e_gmm"); e_mm = S("e_mm")
    e_gev = S("e_gev"); e_fev = S("e_fev"); e_fst = SA("e_fst", 4)

    do_c = "C" in phases
    do_e = "E" in phases
    if do_c or do_e:
        cest = ExitStack()
        csb = lambda n, sh, dt: cest.enter_context(nc.sbuf_tensor(n, sh, dt))
        if do_c:
            kbuf = csb("kbuf", [128, NCH * T], BF16)
            vtmp = [csb(f"vtmp{i}", [128, T], BF16) for i in range(2)]
            ksq = [csb(f"ksq{i}", [128, T], BF16) for i in range(NCH)]
            nrm = csb("nrm", [1, T], F32)
            inv = csb("inv", [1, T], F32)
            invb = csb("invb", [128, T], F32)
            ptmp = csb("ptmp", [128, T], BF16)
            pp = csb("pp", [128, T], F32)
            acol = csb("acol", [128, 1], F32)
            psn = cest.enter_context(nc.psum_tensor("psn", [1, T], F32))
            psbc = cest.enter_context(nc.psum_tensor("psbc", [128, T], F32))

        if do_e:
            wgbuf = [csb(f"wgbuf{i}", [128, NCH * 128], BF16) for i in range(3)]
            wubuf = [csb(f"wubuf{i}", [128, NCH * 128], BF16) for i in range(3)]
            gate_s = [csb(f"gate{i}", [128, T], F32) for i in range(2)]
            fstg = [csb(f"fstg{i}", [128, T], BF16) for i in range(4)]
            psG = [cest.enter_context(nc.psum_tensor(f"psG{i}", [128, T], F32)) for i in range(3)]
            psU = [cest.enter_context(nc.psum_tensor(f"psU{i}", [128, T], F32)) for i in range(3)]

        cpi_box = [0]

        def emit_C_kloads(ts):
            for c in range(NCH):
                p.emit("scalar", lambda s, c=c, ts=ts: s.dma_start(
                           out=kbuf[:, c*T:(c+1)*T], in_=k_s[c][:, ts*T:(ts+1)*T]),
                       waits=[(c_dve, 1 + ts * NCH)], inc=c_kld, incv=16)

        def emit_C_ksq(ts):
            for c in range(NCH):
                p.emit("vector", lambda v, c=c: v.tensor_mul(
                           ksq[c][:], kbuf[:, c*T:(c+1)*T], kbuf[:, c*T:(c+1)*T]),
                       waits=[(c_kld, 256 * (ts + 1)), (c_nmm, NCH * ts)],
                       inc=c_ksq)

        def emit_C_norm(ts):
            for c in range(NCH):
                p.emit("tensor", lambda t, c=c: t.matmul(
                           psn[:], ones_bf[:], ksq[c][:],
                           start=(c == 0), stop=(c == NCH - 1)),
                       waits=[(c_ksq, ts * NCH + c + 1), (c_nrm, ts)],
                       inc=c_nmm)
            p.emit("scalar", lambda sc: sc.activation(nrm[:], psn[:], AF.Sqrt),
                   waits=[(c_nmm, 16 * (ts + 1))], inc=c_nrm)
            p.emit("vector", lambda v: v.tensor_scalar(nrm[:], nrm[:], 1e-12, None, ALU.max),
                   waits=[(c_nrm, ts + 1)], inc=c_mx)
            p.emit("vector", lambda v: v.reciprocal(inv[:], nrm[:]),
                   waits=[(c_mx, ts + 1)], inc=c_inv)

        def emit_C_p_pre(ts):
            p.emit("tensor", lambda t: t.matmul(psbc[:], ones_f32[:], inv[:], start=True, stop=True),
                   waits=[(c_inv, ts + 1), (c_invb, ts)], inc=c_bmm)
            p.emit("vector", lambda v: v.tensor_copy(invb[:], psbc[:]),
                   waits=[(c_bmm, ts + 1), (c_red, 16 * ts)], inc=c_invb)

        def emit_C_p_chunks(ts, c0, ncl):
            for c in range(c0, c0 + ncl):
                cpi = cpi_box[0]
                p.emit("scalar", lambda s, c=c, ts=ts, sl=cpi % 2: s.dma_start(
                           out=vtmp[sl][:], in_=v_s[c][:, ts*T:(ts+1)*T]),
                       waits=[(c_dve, cpi)], inc=c_vld[cpi % 2], incv=16)
                p.emit("vector", lambda v, c=c, sl=cpi % 2: v.tensor_mul(
                           ptmp[:], kbuf[:, c*T:(c+1)*T], vtmp[sl][:]),
                       waits=[(c_vld[cpi % 2], 16 * (cpi // 2 + 1)), (c_pc, 2 * cpi)],
                       inc=c_pc)
                p.emit("vector", lambda v: v.scalar_tensor_tensor(
                           pp[:], ptmp[:], 1.0, invb[:], ALU.mult, ALU.mult),
                       waits=[(c_pc, 2 * cpi + 1), (c_invb, ts + 1), (c_red, cpi)],
                       inc=c_pc)
                p.emit("vector", lambda v: v.reduce_sum(acol[:], pp[:], AX.X),
                       waits=[(c_pc, 2 * cpi + 2)], inc=c_red)
                p.emit("vector", lambda v, c=c: v.tensor_add(A_acc[:, c:c+1], A_acc[:, c:c+1], acol[:]),
                       waits=[(c_red, cpi + 1)], inc=c_dve)
                cpi_box[0] = cpi + 1

        def emit_C_p(ts):
            emit_C_p_pre(ts)
            emit_C_p_chunks(ts, 0, NCH)

        def emit_C_fin1():
            p.emit("tensor", lambda t: t.transpose(psbc[0:NCH, 0:128], A_acc[:], id_f32[:]),
                   waits=[(c_dve, c_dve.n)], inc=c_fin)
            p.emit("vector", lambda v: v.tensor_copy(att[:], psbc[0:NCH, 0:128]),
                   waits=[(c_fin, 1)], inc=c_fin)
            p.emit("gpsimd", lambda g: g.dma_start(out=cc_a.ap(), in_=att[:]),
                   waits=[(c_fin, 2)], inc=c_af1, incv=16)
            p.emit("gpsimd", lambda g: g.collective_compute(
                       "AllReduce", ALU.add,
                       replica_groups=[[0, 1], [2, 3], [4, 5], [6, 7]],
                       ins=[cc_a.ap().opt()], outs=[cc_b.ap().opt()]),
                   waits=[(c_af1, 16)], inc=c_cc)

        def emit_C_fin2():
            p.emit("scalar", lambda s: s.dma_start(out=af_t[:], in_=cc_b.ap()),
                   waits=[(c_cc, 1)], inc=c_af2, incv=16)
            p.emit("tensor", lambda t: t.transpose(psbc[:, 0:NCH], af_t[:], id_f32[0:NCH, 0:NCH]),
                   waits=[(c_af2, 16)], inc=c_fin)
            p.emit("vector", lambda v: v.tensor_copy(A_full[:], psbc[:, 0:NCH]),
                   waits=[(c_fin, 3)], inc=c_af)
            if a_dbg is not None:
                p.emit("sync", lambda s: s.dma_start(out=a_dbg, in_=A_full[:]),
                       waits=[(c_af, 1)], inc=c_af3, incv=16)

        if do_c:
            p.emit("vector", lambda v: v.memset(A_acc[:], 0.0), inc=c_dve)

        if do_e:
            for oc in range(NOC):
                blk_i, ph = oc // 8, oc % 8
                cj, cph = (oc // 5, oc % 5) if oc < 40 else (None, None)
                if do_c and cj is not None:
                    if cph == 0:
                        emit_C_kloads(cj)
                    elif cph == 1:
                        emit_C_ksq(cj)
                    elif cph == 2:
                        emit_C_norm(cj)
                    elif cph == 3:
                        emit_C_p_pre(cj)
                if do_c and oc == 42:
                    emit_C_fin1()
                if do_c and oc == 50:
                    emit_C_fin2()
                if "A" in phases and oc in (1, 9, 17, 25):
                    c0 = (oc // 8) * 4
                    p.emit("sync", lambda s, c0=c0: s.dma_start(
                               out=xn_s[c0:c0+4].rearrange("c p t -> p c t"),
                               in_=xn_all[:, c0*SC:(c0+4)*SC]),
                           waits=[(a_evS, NG)], inc=a_st, incv=16)
                p.emit("gpsimd", lambda g, oc=oc: g.dma_start(out=wgbuf[oc % 3][:], in_=wg[oc]),
                       waits=[(e_mm, 8 * (oc - 2))], inc=e_wld[oc % 3], incv=16)
                p.emit("gpsimd", lambda g, oc=oc: g.dma_start(out=wubuf[oc % 3][:], in_=wu[oc]),
                       inc=e_wld[oc % 3], incv=16)
                for ts in range(NT):
                    m = oc * NT + ts
                    for c in range(NCH):
                        p.emit("tensor", lambda t, oc=oc, c=c, ts=ts, m=m: t.matmul(
                                   psG[m % 3][:], wgbuf[oc % 3][:, c*128:(c+1)*128],
                                   xn_all[:, c*SC + ts*T: c*SC + (ts+1)*T],
                                   start=(c == 0), stop=(c == NCH - 1)),
                               waits=[(e_wld[oc % 3], 32 * (oc // 3 + 1)), (e_gev, m - 2)],
                               inc=(e_gmm if c == NCH - 1 else None))
                    for c in range(NCH):
                        p.emit("tensor", lambda t, oc=oc, c=c, ts=ts, m=m: t.matmul(
                                   psU[m % 3][:], wubuf[oc % 3][:, c*128:(c+1)*128],
                                   xn_all[:, c*SC + ts*T: c*SC + (ts+1)*T],
                                   start=(c == 0), stop=(c == NCH - 1)),
                               waits=[(e_fev, m - 1)],
                               inc=(e_mm if c == NCH - 1 else None))
                    p.emit("scalar", lambda sc, oc=oc, m=m: sc.activation(
                               gate_s[m % 2][:], psG[m % 3][:], AF.Silu, scale=sg_t[:, oc:oc+1]),
                           waits=[(e_gmm, m + 1), (e_fev, m - 1)], inc=e_gev)
                    p.emit("vector", lambda v, oc=oc, m=m: v.scalar_tensor_tensor(
                               fstg[m % 4][:], psU[m % 3][:], su_t[:, oc:oc+1], gate_s[m % 2][:],
                               ALU.mult, ALU.mult),
                           waits=[(e_mm, m + 1), (e_gev, m + 1), (e_fst[m % 4], 16 * (m // 4))],
                           inc=e_fev)
                    p.emit("sync", lambda s, oc=oc, ts=ts, m=m: s.dma_start(
                               out=fuse_s[ts][:, oc*T:(oc+1)*T], in_=fstg[m % 4][:]),
                           waits=[(e_fev, m + 1)], inc=e_fst[m % 4], incv=16)
                    if do_c and cj is not None and cph == 3:
                        emit_C_p_chunks(cj, ts, 1)
                    if do_c and cj is not None and cph == 4:
                        emit_C_p_chunks(cj, 8 + ts, 1)
        elif do_c:
            for ts in range(NT):
                emit_C_kloads(ts)
                emit_C_ksq(ts)
                emit_C_norm(ts)
                emit_C_p(ts)
            emit_C_fin1()
            emit_C_fin2()
        cest.close()
        p.marker(c_dve, c_af, c_nmm, c_bmm, c_fin, c_kld, *c_vld, c_af1, c_af2, c_af3)
        p.marker(e_mm, e_gmm, e_gev, e_fev, *e_fst, *e_wld, a_st)

    d_wl = SA("d_wl", 4); d_sc = S("d_sc"); d_qld = SA("d_qld", 3)
    d_mm = S("d_mm"); d_ev = S("d_ev"); d_st = SA("d_st", 2)
    f_fld = SA("f_fld", 2)

    if "D" in phases:
        dst_ = ExitStack()
        dsb = lambda n, sh, dt: dst_.enter_context(nc.sbuf_tensor(n, sh, dt))
        woutS = big[:, NOC * T: 2 * NOC * T]
        qsl = [dsb(f"qsl{i}", [128, NCH * T], BF16) for i in range(3)]
        dstg = [dsb(f"dstg{i}", [128, T], BF16) for i in range(2)]
        psD = [dst_.enter_context(nc.psum_tensor(f"psD{i}", [128, T], F32)) for i in range(4)]

        def emit_qsl(ts):
            for c in range(NCH):
                eng = "scalar" if c % 2 == 0 else "sync"
                p.emit(eng, lambda s, c=c, ts=ts: s.dma_start(
                           out=qsl[ts % 3][:, c*T:(c+1)*T], in_=q_s[c][:, ts*T:(ts+1)*T]),
                       waits=[(d_mm, NCH * (ts - 2))], inc=d_qld[ts % 3], incv=16)

        emit_qsl(0)
        e_done = e_mm.n
        for ec in range(4):
            p.emit("gpsimd", lambda g, ec=ec: g.dma_start(
                       out=woutS[:, ec*2048:(ec+1)*2048], in_=wout[ec]),
                   waits=[(d_wl[ec % 4], 16 * (ec // 4)), (e_mm, e_done)],
                   inc=d_wl[ec % 4], incv=16)
        emit_qsl(1)
        for ec in range(4, NCH):
            p.emit("gpsimd", lambda g, ec=ec: g.dma_start(
                       out=woutS[:, ec*2048:(ec+1)*2048], in_=wout[ec]),
                   waits=[(d_wl[ec % 4], 16 * (ec // 4))], inc=d_wl[ec % 4], incv=16)
        emit_qsl(2)
        for ts in range(NT):
            if ts >= 1 and ts + 2 < NT:
                emit_qsl(ts + 2)
            for c in range(NCH):
                p.emit("vector", lambda v, c=c, ts=ts: v.tensor_scalar_mul(
                           qsl[ts % 3][:, c*T:(c+1)*T], qsl[ts % 3][:, c*T:(c+1)*T],
                           A_full[:, c:c+1]),
                       waits=[(d_qld[ts % 3], 256 * (ts // 3 + 1)), (c_af, 1)],
                       inc=(d_sc if c == NCH - 1 else None))
            for ec in range(NCH):
                m = ts * NCH + ec
                for c in range(NCH):
                    p.emit("tensor", lambda t, ec=ec, c=c, ts=ts, m=m: t.matmul(
                               psD[m % 4][:], woutS[:, ec*2048 + c*128: ec*2048 + (c+1)*128],
                               qsl[ts % 3][:, c*T:(c+1)*T],
                               start=(c == 0), stop=(c == NCH - 1)),
                           waits=[(d_sc, ts + 1), (d_wl[ec % 4], 16 * (ec // 4 + 1)), (d_ev, m - 3)],
                           inc=(d_mm if c == NCH - 1 else None))
                p.emit("scalar", lambda sc, ec=ec, m=m: sc.activation(
                           dstg[m % 2][:], psD[m % 4][:], AF.Identity, bias=bo_t[:, ec:ec+1]),
                       waits=[(d_mm, m + 1), (d_st[m % 2], 16 * (m // 2))], inc=d_ev)
                p.emit("sync", lambda s, ec=ec, ts=ts, m=m: s.dma_start(
                           out=yo_s[ec][:, ts*T:(ts+1)*T], in_=dstg[m % 2][:]),
                       waits=[(d_ev, m + 1)], inc=d_st[m % 2], incv=16)
            if "F" in phases and ts < 1:
                FH0 = NOC * T
                for half in range(2):
                    p.emit("scalar", lambda s, tsj=ts, half=half: s.dma_start(
                               out=big[:, (tsj % 2)*FH0 + half*(FH0//2): (tsj % 2)*FH0 + (half+1)*(FH0//2)],
                               in_=fuse_s[tsj][:, half*(FH0//2):(half+1)*(FH0//2)]),
                           inc=f_fld[ts % 2], incv=16)
        dst_.close()
        p.marker(d_mm, d_ev, *d_st, d_sc, *d_qld)

    f_xld = S("f_xld"); f_wld = SA("f_wld", 8); f_mm = S("f_mm")
    f_yld = SA("f_yld", 2); f_xnld = SA("f_xnld", 2); f_r2 = S("f_r2"); f_r3 = S("f_r3")
    f_tr = S("f_tr"); f_ev2 = S("f_ev2"); f_out = SA("f_out", 4)

    if "F" in phases:
        fst = ExitStack()
        fsb = lambda n, sh, dt: fst.enter_context(nc.sbuf_tensor(n, sh, dt))
        wdbuf = [fsb(f"wdbuf{i}", [128, NOC * 128], BF16) for i in range(2)]
        stage = [fsb(f"stage{i}", [128, D], F32) for i in range(4)]
        yo_b = [fsb(f"yob{i}", [128, T], BF16) for i in range(2)]
        xn_b = [fsb(f"xnb{i}", [128, T], BF16) for i in range(2)]
        r3 = [fsb(f"r3_{i}", [128, T], F32) for i in range(3)]
        psF = [fst.enter_context(nc.psum_tensor(f"psF{i}", [128, T], F32)) for i in range(3)]
        psT = [fst.enter_context(nc.psum_tensor(f"psTF{i}", [128, T], F32)) for i in range(3)]

        def emit_tp_adds(hm):
            for g in range(4):
                p.emit("tensor", lambda t, g=g, hm=hm: t.transpose(
                           psT[hm % 3][:, g*128:(g+1)*128], r3[hm % 3][:, g*128:(g+1)*128],
                           id_f32[:]),
                       waits=[(f_r3, hm + 1), (f_ev2, hm - 2)],
                       inc=(f_tr if g == 3 else None))
            h = hm % NCH
            tsi = hm // NCH
            for g in range(4):
                p.emit("vector", lambda v, g=g, h=h, hm=hm: v.tensor_add(
                           stage[g][:, h*128:(h+1)*128], psT[hm % 3][:, g*128:(g+1)*128],
                           stage[g][:, h*128:(h+1)*128]),
                       waits=[(f_tr, hm + 1), (f_xld, 16 * (tsi * 4 + 4))],
                       inc=(f_ev2 if g == 3 else None))

        FH = NOC * T

        def emit_fuse_load(tsj):
            fbj = (tsj % 2) * FH
            for half in range(2):
                p.emit("scalar", lambda s, tsj=tsj, half=half, fbj=fbj: s.dma_start(
                           out=fuse_ts[:, fbj + half*(FH//2): fbj + (half+1)*(FH//2)],
                           in_=fuse_s[tsj][:, half*(FH//2):(half+1)*(FH//2)]),
                       waits=[(f_mm, NCH * (tsj - 1) if tsj >= 2 else 0)], inc=f_fld[tsj % 2], incv=16)

        if "D" not in phases:
            emit_fuse_load(0)
        emit_fuse_load(1)
        for tsi in range(NT):
            fb = (tsi % 2) * FH
            for h in range(NCH):
                if h == 1:
                    for g in range(4):
                        p.emit("sync", lambda s, tsi=tsi, g=g: s.dma_start(
                                   out=stage[g][:], in_=x[tsi*T + g*128: tsi*T + (g+1)*128, :]),
                               waits=[(f_out[g], 16 * tsi)], inc=f_xld, incv=16)
                hm = tsi * NCH + h
                for q in range(4):
                    p.emit("gpsimd", lambda g, h=h, hm=hm, q=q: g.dma_start(
                               out=wdbuf[hm % 2][:, q*16*128:(q+1)*16*128], in_=wd[h, :, q*16:(q+1)*16]),
                           waits=[(f_mm, hm - 1)], inc=f_wld[(hm % 2) * 4 + q], incv=16)
                p.emit("sync", lambda s, h=h, tsi=tsi, hm=hm: s.dma_start(
                           out=yo_b[hm % 2][:], in_=yo_s[h][:, tsi*T:(tsi+1)*T]),
                       waits=[(f_r2, hm - 1)], inc=f_yld[hm % 2], incv=16)
                p.emit("sync", lambda s, h=h, tsi=tsi, hm=hm: s.dma_start(
                           out=xn_b[hm % 2][:], in_=xn_s[h][:, tsi*T:(tsi+1)*T]),
                       waits=[(f_r3, hm - 1)], inc=f_xnld[hm % 2], incv=16)
                if hm >= 2:
                    emit_tp_adds(hm - 2)
                for oc in range(NOC):
                    p.emit("tensor", lambda t, oc=oc, hm=hm, tsi=tsi, fb=fb: t.matmul(
                               psF[hm % 3][:], wdbuf[hm % 2][:, oc*128:(oc+1)*128],
                               fuse_ts[:, fb + oc*T: fb + (oc+1)*T],
                               start=(oc == 0), stop=(oc == NOC - 1)),
                           waits=[(f_wld[(hm % 2) * 4 + oc // 16], 16 * (hm // 2 + 1)),
                                  (f_fld[tsi % 2], 32 * (tsi // 2 + 1)),
                                  (f_r2, hm - 2)],
                           inc=(f_mm if oc == NOC - 1 else None))
                p.emit("vector", lambda v, h=h, hm=hm: v.scalar_tensor_tensor(
                           r3[hm % 3][:], psF[hm % 3][:], sd_t[:, h:h+1], yo_b[hm % 2][:],
                           ALU.mult, ALU.add),
                       waits=[(f_mm, hm + 1), (f_yld[hm % 2], 16 * (hm // 2 + 1)), (f_tr, hm - 2)],
                       inc=f_r2)
                p.emit("vector", lambda v, hm=hm: v.tensor_add(
                           r3[hm % 3][:], r3[hm % 3][:], xn_b[hm % 2][:]),
                       waits=[(f_xnld[hm % 2], 16 * (hm // 2 + 1)), (f_r2, hm + 1)], inc=f_r3)
            if tsi + 2 < NT:
                emit_fuse_load(tsi + 2)
            for g in range(4):
                p.emit("scalar", lambda s, tsi=tsi, g=g: s.dma_start(
                           out=out[tsi*T + g*128: tsi*T + (g+1)*128, :], in_=stage[g][:]),
                       waits=[(f_ev2, (tsi + 1) * NCH)], inc=f_out[g], incv=16)
        emit_tp_adds(NT * NCH - 2)
        emit_tp_adds(NT * NCH - 1)
        fst.close()

    p.emit("sync", None, waits=[(s_, s_.n) for s_ in all_sems if s_.n > 0])

    with nc.Block() as blk:
        @blk.tensor
        def _(t):
            p.replay(t, "tensor")
        @blk.vector
        def _(v):
            p.replay(v, "vector")
        @blk.scalar
        def _(sc):
            p.replay(sc, "scalar")
        @blk.sync
        def _(s):
            p.replay(s, "sync")
        @blk.gpsimd
        def _(g):
            p.replay(g, "gpsimd")
    gst.close()
    return nc


def _prep_weights(L0_attn_W_in, L0_attn_b_in, L0_attn_W_out, L0_attn_b_out,
                  L0_ffn_w_gate, L0_ffn_w_up, L0_ffn_w_down,
                  L0_ffn_s_gate, L0_ffn_s_up, L0_ffn_s_down):
    perm = np.concatenate([np.arange(0, 1024, 2), np.arange(1, 1024, 2),
                           np.arange(1024, 2048)])
    W_in = np.asarray(L0_attn_W_in, np.float32)
    Win_p = np.concatenate([W_in[0:2048][perm], W_in[2048:4096][perm],
                            W_in[4096:6144][perm]], axis=0)
    win = np.ascontiguousarray(Win_p.reshape(NEC, 128, NCH, 128).transpose(0, 3, 2, 1))
    b_in = np.asarray(L0_attn_b_in, np.float32)
    b_in_p = np.concatenate([b_in[0:2048][perm], b_in[2048:4096][perm],
                             b_in[4096:6144][perm]])
    b_in_t = np.ascontiguousarray(b_in_p.reshape(NEC, 128).T)
    Wt = np.asarray(L0_attn_W_out, np.float32)[:, perm].T
    wout = np.ascontiguousarray(Wt.reshape(NCH, 128, NCH, 128).transpose(2, 1, 0, 3))
    b_out_t = np.ascontiguousarray(np.asarray(L0_attn_b_out, np.float32).reshape(NCH, 128).T)
    wg_ = np.asarray(L0_ffn_w_gate, np.int8)
    wu_ = np.asarray(L0_ffn_w_up, np.int8)
    wd_ = np.asarray(L0_ffn_w_down, np.int8)
    wg = np.ascontiguousarray(wg_.reshape(NCH, 128, NOC, 128).transpose(2, 1, 0, 3))
    wu = np.ascontiguousarray(wu_.reshape(NCH, 128, NOC, 128).transpose(2, 1, 0, 3))
    wd = np.ascontiguousarray(wd_.reshape(NOC, 128, NCH, 128).transpose(2, 1, 0, 3))
    sg = np.ascontiguousarray(np.asarray(L0_ffn_s_gate, np.float32).reshape(NOC, 128).T)
    su = np.ascontiguousarray(np.asarray(L0_ffn_s_up, np.float32).reshape(NOC, 128).T)
    sd = np.ascontiguousarray(np.asarray(L0_ffn_s_down, np.float32).reshape(NCH, 128).T)
    return dict(win=win, b_in=b_in_t, wout=wout, b_out=b_out_t,
                wg=wg, wu=wu, wd=wd, sg=sg, su=su, sd=sd)


def _rope_tables(h):
    pos = (np.float32(h * SC) + np.arange(SC, dtype=np.float32))
    invf = (np.float32(10000.0) ** (-(np.arange(0, 1024, 2, dtype=np.float32)
                                      / np.float32(1024.0))))
    fr = invf[:, None].astype(np.float32) * pos[None, :].astype(np.float32)
    return (np.ascontiguousarray(np.cos(fr).astype(np.float32).reshape(4, 128, SC)),
            np.ascontiguousarray(np.sin(fr).astype(np.float32).reshape(4, 128, SC)))


_CACHE = {}

def kernel(L0_attn_X, L0_attn_W_in, L0_attn_b_in, L0_attn_W_out, L0_attn_b_out,
           L0_ffn_w_gate, L0_ffn_w_up, L0_ffn_w_down,
           L0_ffn_s_gate, L0_ffn_s_up, L0_ffn_s_down):
    X = np.asarray(L0_attn_X, np.float32)
    w = _prep_weights(L0_attn_W_in, L0_attn_b_in, L0_attn_W_out, L0_attn_b_out,
                      L0_ffn_w_gate, L0_ffn_w_up, L0_ffn_w_down,
                      L0_ffn_s_gate, L0_ffn_s_up, L0_ffn_s_down)
    tabs = [_rope_tables(0), _rope_tables(1)]
    in_maps = []
    for c in range(8):
        b, h = c // 2, c % 2
        m = dict(w)
        m["x"] = np.ascontiguousarray(X[b, h*SC:(h+1)*SC, :])
        m["cosb"], m["sinb"] = tabs[h]
        in_maps.append(m)
    if "nc" not in _CACHE:
        _CACHE["nc"] = build_program()
    res = run_bass_kernel_spmd(_CACHE["nc"], in_maps, core_ids=list(range(8)))
    outp = np.empty((4, 8192, D), np.float32)
    for c in range(8):
        b, h = c // 2, c % 2
        outp[b, h*SC:(h+1)*SC, :] = res.results[c]["out"]
    return outp


# version 21
# speedup vs baseline: 1.1813x; 1.1813x over previous
import numpy as np
import ml_dtypes
from contextlib import ExitStack

import concourse.bass as bass
import concourse.mybir as mybir
from concourse.bass_utils import run_bass_kernel_spmd

F32 = mybir.dt.float32
BF16 = mybir.dt.bfloat16
I8 = mybir.dt.int8
AF = mybir.ActivationFunctionType
ALU = mybir.AluOpType
AX = mybir.AxisListType

D = 2048
NCH = 16
SC = 4096
T = 512
NT = SC // T
NEC = 48
NOC = 64
NG = SC // 128
EPS = 1e-5

ENGINES = ("tensor", "vector", "scalar", "sync", "gpsimd")


class Sem:
    __slots__ = ("h", "n", "name")
    def __init__(self, h, name):
        self.h = h
        self.n = 0
        self.name = name


class Prog:
    def __init__(self):
        self.ops = {e: [] for e in ENGINES}
        self.lastwait = {e: {} for e in ENGINES}
        self.pending = []

    def marker(self, *sems):
        self.pending += [(s, s.n) for s in sems]

    def emit(self, eng, fn, waits=(), inc=None, incv=1, no_pending=False):
        ws = []
        for (sem, val) in (list(waits) if no_pending else list(self.pending) + list(waits)):
            if val <= 0:
                continue
            lw = self.lastwait[eng].get(id(sem), 0)
            if val > lw:
                ws.append((sem, val))
                self.lastwait[eng][id(sem)] = val
        if inc is not None:
            inc.n += incv
        self.ops[eng].append((fn, ws, inc, incv))
        return inc.n if inc is not None else None

    def replay(self, eng, name):
        for fn, ws, inc, incv in self.ops[name]:
            for sem, val in ws:
                eng.wait_ge(sem.h, val)
            if fn is None:
                continue
            ins = fn(eng)
            if inc is not None:
                ins.then_inc(inc.h, incv)


def make_identity(nc, identity):
    sq = identity.shape[0]
    nc.gpsimd.memset(identity, 0.0)
    return nc.gpsimd.affine_select(
        out=identity, in_=identity, compare_op=ALU.not_equal,
        fill=1.0, base=0, pattern=[[-1, sq]], channel_multiplier=1)


def build_program(phases="ABCEDF", dump=()):
    nc = bass.Bass("TRN2", target_bir_lowering=False)
    p = Prog()

    def dram(name, shape, dtype, kind="Internal"):
        if name in dump:
            kind = "ExternalOutput"
        return nc.dram_tensor(name, shape, dtype, kind=kind).ap()

    x = nc.dram_tensor("x", [SC, D], F32, kind="ExternalInput").ap()
    win = nc.dram_tensor("win", [NEC, 128, NCH, 128], F32, kind="ExternalInput").ap()
    b_in = nc.dram_tensor("b_in", [128, NEC], F32, kind="ExternalInput").ap()
    wout = nc.dram_tensor("wout", [NCH, 128, NCH, 128], F32, kind="ExternalInput").ap()
    b_out = nc.dram_tensor("b_out", [128, NCH], F32, kind="ExternalInput").ap()
    wg = nc.dram_tensor("wg", [NOC, 128, NCH, 128], I8, kind="ExternalInput").ap()
    wu = nc.dram_tensor("wu", [NOC, 128, NCH, 128], I8, kind="ExternalInput").ap()
    wd = nc.dram_tensor("wd", [NCH, 128, NOC, 128], I8, kind="ExternalInput").ap()
    sg = nc.dram_tensor("sg", [128, NOC], F32, kind="ExternalInput").ap()
    su = nc.dram_tensor("su", [128, NOC], F32, kind="ExternalInput").ap()
    sd = nc.dram_tensor("sd", [128, NCH], F32, kind="ExternalInput").ap()
    cosb = nc.dram_tensor("cosb", [4, 128, SC], F32, kind="ExternalInput").ap()
    sinb = nc.dram_tensor("sinb", [4, 128, SC], F32, kind="ExternalInput").ap()
    out = nc.dram_tensor("out", [SC, D], F32, kind="ExternalOutput").ap()

    xn_s = dram("xn_s", [NCH, 128, SC], BF16)
    q_s = dram("q_s", [NCH, 128, SC], BF16)
    k_s = dram("k_s", [NCH, 128, SC], BF16)
    v_s = dram("v_s", [NCH, 128, SC], BF16)
    fuse_s = dram("fuse_s", [NT, 128, NOC * T], BF16)
    yo_s = dram("yo_s", [NCH, 128, SC], BF16)
    wout_b = dram("wout_b", [NCH, 128, NCH, 128], BF16)
    cc_a = nc.dram_tensor("cc_a", [NCH, 128], F32)
    cc_b = nc.dram_tensor("cc_b", [NCH, 128], F32)
    a_dbg = dram("a_dbg", [128, NCH], F32) if "a_dbg" in dump else None

    gst = ExitStack()
    sb = lambda name, shape, dt: gst.enter_context(nc.sbuf_tensor(name, shape, dt))
    ps = lambda name, shape, dt: gst.enter_context(nc.psum_tensor(name, shape, dt))
    mksem = lambda name: Sem(gst.enter_context(nc.semaphore(name)), name)

    id_bf = sb("id_bf", [128, 128], BF16)
    id_f32 = sb("id_f32", [128, 128], F32)
    ones_bf = sb("ones_bf", [128, 1], BF16)
    ones_f32 = sb("ones_f32", [1, 128], F32)
    b_in_t = sb("b_in_t", [128, NEC], F32)
    bo_t = sb("bo_t", [128, NCH], F32)
    sg_t = sb("sg_t", [128, NOC], F32)
    su_t = sb("su_t", [128, NOC], F32)
    sd_t = sb("sd_t", [128, NCH], F32)
    A_full = sb("A_full", [128, NCH], F32)
    eps_t = sb("eps_t", [128, 1], F32)
    neghalf_t = sb("neghalf_t", [128, 1], F32)
    A_acc = sb("A_acc", [128, NCH], F32)
    att = sb("att", [NCH, 128], F32)
    af_t = sb("af_t", [NCH, 128], F32)

    all_sems = []
    def S(name):
        s = mksem(name)
        all_sems.append(s)
        return s

    def SA(name, n):
        return [S(f"{name}{i}") for i in range(n)]

    g_const = S("g_const")
    s_const = S("s_const")

    for idt in (id_bf, id_f32):
        p.emit("gpsimd", lambda g, idt=idt: g.memset(idt[:, :], 0.0), inc=g_const)
        p.emit("gpsimd", lambda g, idt=idt: g.affine_select(
                   out=idt[:, :], in_=idt[:, :], compare_op=ALU.not_equal,
                   fill=1.0, base=0, pattern=[[-1, 128]], channel_multiplier=1),
               waits=[(g_const, g_const.n)], inc=g_const)
    p.emit("gpsimd", lambda g: g.memset(ones_bf[:], 1.0), inc=g_const)
    p.emit("gpsimd", lambda g: g.memset(ones_f32[:], 1.0), inc=g_const)
    p.emit("gpsimd", lambda g: g.memset(eps_t[:], EPS), inc=g_const)
    p.emit("gpsimd", lambda g: g.memset(neghalf_t[:], -0.5), inc=g_const)
    for dst, src in ((b_in_t, b_in), (bo_t, b_out), (sg_t, sg), (su_t, su), (sd_t, sd)):
        p.emit("sync", lambda s, dst=dst, src=src: s.dma_start(out=dst[:], in_=src),
               inc=s_const, incv=16)
    p.marker(g_const, s_const)

    big = sb("big", [128, NCH * SC], BF16)
    xn_all = big
    y_all = big
    fuse_ts = big

    a_xld = SA("a_xld", 3); a_bns = S("a_bns"); a_mv = S("a_mv"); a_sq = S("a_sq"); a_ln = S("a_ln")
    a_xn = S("a_xn")
    a_tr = S("a_tr"); a_evS = S("a_evS"); a_evV = S("a_evV"); a_st = S("a_st")


    b_wld = SA("b_wld", 3); b_mm = S("b_mm"); b_evac = S("b_evac"); b_rp = S("b_rp")
    d_wcast = S("d_wcast")
    b_csld = SA("b_csld", 3); b_st = SA("b_st", 4); b_rt = S("b_rt")
    ab_st = ExitStack()
    bgen = None
    if "B" in phases:
        bst = ab_st
        bsb = lambda n, sh, dt: bst.enter_context(nc.sbuf_tensor(n, sh, dt))
        wbuf = [bsb(f"wbufB{i}", [128, NCH * 128], BF16) for i in range(3)]
        lead_buf = [bsb(f"lead_buf{i}", [128, SC], BF16) for i in range(2)]
        trail_tmp = [bsb(f"trail{i}", [128, T], BF16) for i in range(4)]
        cs_cos = [bsb(f"cscos{i}", [128, T], BF16) for i in range(3)]
        cs_sin = [bsb(f"cssin{i}", [128, T], BF16) for i in range(3)]
        rtmp1 = bsb("rtmp1", [128, T], BF16)
        rtmp2 = bsb("rtmp2", [128, T], BF16)
        stg = [bsb(f"stgB{i}", [128, T], BF16) for i in range(4)]
        psB = [bst.enter_context(nc.psum_tensor(f"psB{i}", [128, T], F32)) for i in range(6)]

        eorder = [0, 4, 1, 5, 2, 6, 3, 7] + list(range(8, 16)) + \
                 [16, 20, 17, 21, 18, 22, 19, 23] + list(range(24, 32)) + list(range(32, 48))
        def chunk_target(ec):
            if ec < 16:
                return q_s, ec
            if ec < 32:
                return k_s, ec - 16
            return v_s, ec - 32

        st_count = 0
        bst_uses = [0, 0, 0, 0]
        cs_count = 0
        pairs_done = 0

        def gen_B():
            nonlocal st_count, cs_count, pairs_done
            for idx, ec in enumerate(eorder):
                is_lead = ec in (0, 1, 2, 3, 16, 17, 18, 19)
                is_trail = ec in (4, 5, 6, 7, 20, 21, 22, 23)
                jc = (ec % 16) % 4
                p.emit("gpsimd", lambda g, ec=ec, idx=idx: g.dma_start(
                           out=wbuf[idx % 3][:], in_=win[ec]),
                       waits=[(b_mm, 8 * (idx - 2)), (g_const, 8)],
                       inc=b_wld[idx % 3], incv=16, no_pending=(idx < 3))
                if idx in (16, 24, 32, 40):
                    j = (idx - 16) // 8
                    p.emit("gpsimd", lambda g, j=j: g.dma_start(
                               out=wout_b[4*j:4*j+4], in_=wout[4*j:4*j+4]),
                           inc=d_wcast, incv=16)
                tgt, tc = chunk_target(ec)
                for ts in range(NT):
                    m = idx * NT + ts
                    bank = m % 6
                    for c in range(NCH):
                        p.emit("tensor", lambda t, idx=idx, bank=bank, c=c, ts=ts: t.matmul(
                                   psB[bank][:],
                                   wbuf[idx % 3][:, c*128:(c+1)*128],
                                   xn_all[:, c*SC + ts*T: c*SC + (ts+1)*T],
                                   start=(c == 0), stop=(c == NCH - 1)),
                               waits=[(b_wld[idx % 3], 16 * (idx // 3 + 1)), (b_evac, m - 5),
                                      (a_evS, (4 * (ts + 1)) if "A" in phases else 0)],
                               inc=(b_mm if c == NCH - 1 else None))
                    if is_lead:
                        p.emit("scalar", lambda sc, bank=bank, ec=ec, ts=ts, lb=pairs_done % 2: sc.activation(
                                   lead_buf[lb][:, ts*T:(ts+1)*T], psB[bank][:], AF.Identity,
                                   bias=b_in_t[:, ec:ec+1]),
                               waits=[(b_mm, m + 1), (b_rp, 16 * (pairs_done - 1))],
                               inc=b_evac)
                    elif is_trail:
                        csl = cs_count % 3
                        p.emit("gpsimd", lambda g, jc=jc, ts=ts, csl=csl: g.dma_start(
                                   out=cs_cos[csl][:], in_=cosb[jc][:, ts*T:(ts+1)*T]),
                               waits=[(b_rp, 2 * (cs_count - 2))], inc=b_csld[csl], incv=16)
                        p.emit("gpsimd", lambda g, jc=jc, ts=ts, csl=csl: g.dma_start(
                                   out=cs_sin[csl][:], in_=sinb[jc][:, ts*T:(ts+1)*T]),
                               inc=b_csld[csl], incv=16)
                        cs_count += 1
                        p.emit("scalar", lambda sc, bank=bank, ec=ec, ts=ts: sc.activation(
                                   trail_tmp[ts % 4][:], psB[bank][:], AF.Identity,
                                   bias=b_in_t[:, ec:ec+1]),
                               waits=[(b_mm, m + 1), (b_rp, 2 * cs_count - 8)],
                               inc=b_evac)
                        slA = st_count % 4
                        slB = (st_count + 1) % 4
                        N0 = cs_count - 1
                        lb = pairs_done % 2
                        p.emit("vector", lambda v, ts=ts, csl=csl, lb=lb: v.tensor_mul(
                                   rtmp1[:], lead_buf[lb][:, ts*T:(ts+1)*T], cs_cos[csl][:]),
                               waits=[(b_evac, b_evac.n), (b_csld[csl], 32 * ((cs_count - 1) // 3 + 1)),
                                      (b_rp, 2 * N0),
                                      (b_st[st_count % 4], 16 * bst_uses[st_count % 4]),
                                      (b_st[(st_count + 1) % 4], 16 * bst_uses[(st_count + 1) % 4])],
                               inc=b_rt)
                        p.emit("vector", lambda v, ts=ts, csl=csl: v.tensor_mul(
                                   rtmp2[:], trail_tmp[ts % 4][:], cs_sin[csl][:]), inc=b_rt)
                        p.emit("vector", lambda v, slA=slA: v.tensor_sub(stg[slA][:], rtmp1[:], rtmp2[:]),
                               waits=[(b_rt, 4 * N0 + 2)], inc=b_rp)
                        p.emit("vector", lambda v, ts=ts, csl=csl: v.tensor_mul(
                                   rtmp1[:], trail_tmp[ts % 4][:], cs_cos[csl][:]),
                               waits=[(b_rp, 2 * N0 + 1)], inc=b_rt)
                        p.emit("vector", lambda v, ts=ts, csl=csl, lb=lb: v.tensor_mul(
                                   rtmp2[:], lead_buf[lb][:, ts*T:(ts+1)*T], cs_sin[csl][:]), inc=b_rt)
                        p.emit("vector", lambda v, slB=slB: v.tensor_add(stg[slB][:], rtmp1[:], rtmp2[:]),
                               waits=[(b_rt, 4 * N0 + 4)], inc=b_rp)
                        ltg, ltc = chunk_target(ec - 4)
                        p.emit("sync", lambda s, ltg=ltg, ltc=ltc, ts=ts, slA=slA: s.dma_start(
                                   out=ltg[ltc][:, ts*T:(ts+1)*T], in_=stg[slA][:]),
                               waits=[(b_rp, b_rp.n)], inc=b_st[slA], incv=16)
                        p.emit("sync", lambda s, tgt=tgt, tc=tc, ts=ts, slB=slB: s.dma_start(
                                   out=tgt[tc][:, ts*T:(ts+1)*T], in_=stg[slB][:]),
                               inc=b_st[slB], incv=16)
                        bst_uses[slA] += 1; bst_uses[slB] += 1
                        st_count += 2
                    else:
                        sl = st_count % 4
                        p.emit("scalar", lambda sc, bank=bank, ec=ec, sl=sl: sc.activation(
                                   stg[sl][:], psB[bank][:], AF.Identity,
                                   bias=b_in_t[:, ec:ec+1]),
                               waits=[(b_mm, m + 1), (b_st[sl], 16 * bst_uses[sl])],
                               inc=b_evac)
                        p.emit("sync", lambda s, tgt=tgt, tc=tc, ts=ts, sl=sl: s.dma_start(
                                   out=tgt[tc][:, ts*T:(ts+1)*T], in_=stg[sl][:]),
                               waits=[(b_evac, b_evac.n)], inc=b_st[sl], incv=16)
                        bst_uses[sl] += 1
                        st_count += 1
                    yield (idx, ts)
                if is_trail:
                    pairs_done += 1
        bgen = gen_B()
    b_next = [0]

    def pump_B(k):
        if bgen is None:
            return
        limit = (6 * (k + 1)) if k >= 0 else 10 ** 9
        while b_next[0] < min(limit, len(eorder) * NT if bgen else 0):
            if k >= 0 and b_next[0] % NT > k:
                break
            if next(bgen, None) is None:
                break
            b_next[0] += 1

    if "A" in phases:
        ast = ab_st
        asb_ = lambda n, sh, dt: ast.enter_context(nc.sbuf_tensor(n, sh, dt))
        xg = [asb_(f"xg{i}", [128, D], F32) for i in range(3)]
        xn_tm = [asb_(f"xn_tm{i}", [128, D], BF16) for i in range(2)]
        bn6 = [asb_(f"bn6{i}", [128, 24], F32) for i in range(3)]
        mv = [asb_(f"mv{i}", [128, 2], F32) for i in range(3)]
        sstd = [asb_(f"sstd{i}", [128, 1], F32) for i in range(3)]
        rstd = [asb_(f"rstd{i}", [128, 1], F32) for i in range(3)]
        psA = [ast.enter_context(nc.psum_tensor("psA0", [128, D], BF16))]

        def emit_xld(g):
            p.emit("scalar", lambda s, g=g: s.dma_start(out=xg[g % 3][:], in_=x[g*128:(g+1)*128, :]),
                   waits=[(a_xn, g - 2)], inc=a_xld[g % 3], incv=16, no_pending=True)

        def emit_a_tail(g):
            p.emit("vector", lambda v, b=g % 3, b3=g % 2: v.tensor_scalar(
                       xn_tm[b3][:], xg[b][:], mv[b][:, 0:1], rstd[b][:], ALU.subtract, ALU.mult),
                   waits=[(a_sq, g + 1), (a_tr, g - 1), (a_xld[g % 3], 16 * (g // 3 + 1))],
                   inc=a_xn)
            for c in range(NCH):
                p.emit("tensor", lambda t, b3=g % 2, c=c: t.transpose(
                           psA[0][:, c*128:(c+1)*128], xn_tm[b3][:, c*128:(c+1)*128], id_bf[:]),
                       waits=[(a_xn, g + 1), (g_const, 1), (a_evS, g)],
                       inc=(a_tr if c == NCH - 1 else None))
            for c in range(NCH):
                p.emit("scalar", lambda sc, c=c, g=g: sc.copy(
                           xn_all[:, c*SC + g*128: c*SC + (g+1)*128], psA[0][:, c*128:(c+1)*128]),
                       waits=[(a_tr, g + 1)], inc=(a_evS if c == NCH - 1 else None))

        for g in range(NG):
            b = g % 3
            if g == 0:
                for gg in range(3):
                    emit_xld(gg)
            p.emit("vector", lambda v, b=b: v.bn_stats(bn6[b][:, 0:6], xg[b][:, 0:512]),
                   waits=[(a_xld[b], 16 * (g // 3 + 1)), (a_mv, g - 2)])
            for i in range(1, 4):
                p.emit("vector", lambda v, b=b, i=i: v.bn_stats(bn6[b][:, 6*i:6*i+6], xg[b][:, 512*i:512*(i+1)]),
                       inc=(a_bns if i == 3 else None))
            p.emit("vector", lambda v, b=b: v.bn_aggr(mv[b][:], bn6[b][:]),
                   waits=[(a_bns, g + 1), (a_ln, g - 2)], inc=a_mv)
            p.emit("scalar", lambda sc, b=b: sc.activation(sstd[b][:], mv[b][:, 1:2], AF.Ln, bias=eps_t[:]),
                   waits=[(a_mv, g + 1)], inc=a_ln)
            p.emit("scalar", lambda sc, b=b: sc.activation(rstd[b][:], sstd[b][:], AF.Exp, scale=neghalf_t[:]),
                   waits=[(a_xn, g - 2)], inc=a_sq)
            if g + 3 < NG:
                emit_xld(g + 3)
            if g >= 1:
                emit_a_tail(g - 1)
        emit_a_tail(NG - 1)
        p.marker(a_evS, a_tr, a_xn, *a_xld)
    if bgen is not None:
        pump_B(-1)
        p.marker(b_mm, b_evac, b_rp, *b_st, *b_csld, *b_wld)

    ab_st.close()

    c_kld = S("c_kld"); c_vld = SA("c_vld", 2); c_ksq = S("c_ksq"); c_nmm = S("c_nmm")
    c_nrm = S("c_nrm"); c_mx = S("c_mx"); c_inv = S("c_inv"); c_bmm = S("c_bmm"); c_invb = S("c_invb"); c_pc = S("c_pc")
    c_dve = S("c_dve"); c_red = S("c_red"); c_fin = S("c_fin"); c_cc = S("c_cc")
    c_af1 = S("c_af1"); c_af2 = S("c_af2"); c_af3 = S("c_af3"); c_af = S("c_af")
    e_wld = SA("e_wld", 3); e_gmm = S("e_gmm"); e_mm = S("e_mm")
    e_gev = S("e_gev"); e_fev = S("e_fev"); e_fst = SA("e_fst", 4)

    do_c = "C" in phases
    do_e = "E" in phases
    if do_c or do_e:
        cest = ExitStack()
        csb = lambda n, sh, dt: cest.enter_context(nc.sbuf_tensor(n, sh, dt))
        if do_c:
            kbuf = csb("kbuf", [128, NCH * T], BF16)
            vtmp = [csb(f"vtmp{i}", [128, T], BF16) for i in range(2)]
            ksq = [csb(f"ksq{i}", [128, T], BF16) for i in range(NCH)]
            nrm = csb("nrm", [1, T], F32)
            inv = csb("inv", [1, T], F32)
            invb = csb("invb", [128, T], F32)
            ptmp = csb("ptmp", [128, T], BF16)
            pp = csb("pp", [128, T], F32)
            acol = csb("acol", [128, 1], F32)
            psn = cest.enter_context(nc.psum_tensor("psn", [1, T], F32))
            psbc = cest.enter_context(nc.psum_tensor("psbc", [128, T], F32))

        if do_e:
            wgbuf = [csb(f"wgbuf{i}", [128, NCH * 128], BF16) for i in range(3)]
            wubuf = [csb(f"wubuf{i}", [128, NCH * 128], BF16) for i in range(3)]
            gate_s = [csb(f"gate{i}", [128, T], F32) for i in range(2)]
            fstg = [csb(f"fstg{i}", [128, T], BF16) for i in range(4)]
            psG = [cest.enter_context(nc.psum_tensor(f"psG{i}", [128, T], F32)) for i in range(3)]
            psU = [cest.enter_context(nc.psum_tensor(f"psU{i}", [128, T], F32)) for i in range(3)]

        cpi_box = [0]

        def emit_C_kloads(ts):
            for c in range(NCH):
                p.emit("scalar", lambda s, c=c, ts=ts: s.dma_start(
                           out=kbuf[:, c*T:(c+1)*T], in_=k_s[c][:, ts*T:(ts+1)*T]),
                       waits=[(c_dve, 1 + ts * NCH)], inc=c_kld, incv=16)

        def emit_C_ksq(ts):
            for c in range(NCH):
                p.emit("vector", lambda v, c=c: v.tensor_mul(
                           ksq[c][:], kbuf[:, c*T:(c+1)*T], kbuf[:, c*T:(c+1)*T]),
                       waits=[(c_kld, 256 * (ts + 1)), (c_nmm, NCH * ts)],
                       inc=c_ksq)

        def emit_C_norm(ts):
            for c in range(NCH):
                p.emit("tensor", lambda t, c=c: t.matmul(
                           psn[:], ones_bf[:], ksq[c][:],
                           start=(c == 0), stop=(c == NCH - 1)),
                       waits=[(c_ksq, ts * NCH + c + 1), (c_nrm, ts)],
                       inc=c_nmm)
            p.emit("scalar", lambda sc: sc.activation(nrm[:], psn[:], AF.Sqrt),
                   waits=[(c_nmm, 16 * (ts + 1))], inc=c_nrm)
            p.emit("vector", lambda v: v.tensor_scalar(nrm[:], nrm[:], 1e-12, None, ALU.max),
                   waits=[(c_nrm, ts + 1)], inc=c_mx)
            p.emit("vector", lambda v: v.reciprocal(inv[:], nrm[:]),
                   waits=[(c_mx, ts + 1)], inc=c_inv)

        def emit_C_p_pre(ts):
            p.emit("tensor", lambda t: t.matmul(psbc[:], ones_f32[:], inv[:], start=True, stop=True),
                   waits=[(c_inv, ts + 1), (c_invb, ts)], inc=c_bmm)
            p.emit("vector", lambda v: v.tensor_copy(invb[:], psbc[:]),
                   waits=[(c_bmm, ts + 1), (c_red, 16 * ts)], inc=c_invb)

        def emit_C_p_chunks(ts, c0, ncl):
            for c in range(c0, c0 + ncl):
                cpi = cpi_box[0]
                p.emit("scalar", lambda s, c=c, ts=ts, sl=cpi % 2: s.dma_start(
                           out=vtmp[sl][:], in_=v_s[c][:, ts*T:(ts+1)*T]),
                       waits=[(c_dve, cpi)], inc=c_vld[cpi % 2], incv=16)
                p.emit("vector", lambda v, c=c, sl=cpi % 2: v.tensor_mul(
                           ptmp[:], kbuf[:, c*T:(c+1)*T], vtmp[sl][:]),
                       waits=[(c_vld[cpi % 2], 16 * (cpi // 2 + 1)), (c_pc, 2 * cpi)],
                       inc=c_pc)
                p.emit("vector", lambda v: v.scalar_tensor_tensor(
                           pp[:], ptmp[:], 1.0, invb[:], ALU.mult, ALU.mult),
                       waits=[(c_pc, 2 * cpi + 1), (c_invb, ts + 1), (c_red, cpi)],
                       inc=c_pc)
                p.emit("vector", lambda v: v.reduce_sum(acol[:], pp[:], AX.X),
                       waits=[(c_pc, 2 * cpi + 2)], inc=c_red)
                p.emit("vector", lambda v, c=c: v.tensor_add(A_acc[:, c:c+1], A_acc[:, c:c+1], acol[:]),
                       waits=[(c_red, cpi + 1)], inc=c_dve)
                cpi_box[0] = cpi + 1

        def emit_C_p(ts):
            emit_C_p_pre(ts)
            emit_C_p_chunks(ts, 0, NCH)

        def emit_C_fin1():
            p.emit("tensor", lambda t: t.transpose(psbc[0:NCH, 0:128], A_acc[:], id_f32[:]),
                   waits=[(c_dve, c_dve.n)], inc=c_fin)
            p.emit("vector", lambda v: v.tensor_copy(att[:], psbc[0:NCH, 0:128]),
                   waits=[(c_fin, 1)], inc=c_fin)
            p.emit("gpsimd", lambda g: g.dma_start(out=cc_a.ap(), in_=att[:]),
                   waits=[(c_fin, 2)], inc=c_af1, incv=16)
            p.emit("gpsimd", lambda g: g.collective_compute(
                       "AllReduce", ALU.add,
                       replica_groups=[[0, 1], [2, 3], [4, 5], [6, 7]],
                       ins=[cc_a.ap().opt()], outs=[cc_b.ap().opt()]),
                   waits=[(c_af1, 16)], inc=c_cc)

        def emit_C_fin2():
            p.emit("scalar", lambda s: s.dma_start(out=af_t[:], in_=cc_b.ap()),
                   waits=[(c_cc, 1)], inc=c_af2, incv=16)
            p.emit("tensor", lambda t: t.transpose(psbc[:, 0:NCH], af_t[:], id_f32[0:NCH, 0:NCH]),
                   waits=[(c_af2, 16)], inc=c_fin)
            p.emit("vector", lambda v: v.tensor_copy(A_full[:], psbc[:, 0:NCH]),
                   waits=[(c_fin, 3)], inc=c_af)
            if a_dbg is not None:
                p.emit("sync", lambda s: s.dma_start(out=a_dbg, in_=A_full[:]),
                       waits=[(c_af, 1)], inc=c_af3, incv=16)

        if do_c:
            p.emit("vector", lambda v: v.memset(A_acc[:], 0.0), inc=c_dve)

        if do_e:
            for oc in range(NOC):
                blk_i, ph = oc // 8, oc % 8
                cj, cph = (oc // 5, oc % 5) if oc < 40 else (None, None)
                if do_c and cj is not None:
                    if cph == 0:
                        emit_C_kloads(cj)
                    elif cph == 1:
                        emit_C_ksq(cj)
                    elif cph == 2:
                        emit_C_norm(cj)
                    elif cph == 3:
                        emit_C_p_pre(cj)
                if do_c and oc == 42:
                    emit_C_fin1()
                if do_c and oc == 50:
                    emit_C_fin2()
                if "A" in phases and oc in (1, 9, 17, 25):
                    c0 = (oc // 8) * 4
                    p.emit("sync", lambda s, c0=c0: s.dma_start(
                               out=xn_s[c0:c0+4].rearrange("c p t -> p c t"),
                               in_=xn_all[:, c0*SC:(c0+4)*SC]),
                           waits=[(a_evS, NG)], inc=a_st, incv=16)
                p.emit("gpsimd", lambda g, oc=oc: g.dma_start(out=wgbuf[oc % 3][:], in_=wg[oc]),
                       waits=[(e_mm, 8 * (oc - 2))], inc=e_wld[oc % 3], incv=16)
                p.emit("gpsimd", lambda g, oc=oc: g.dma_start(out=wubuf[oc % 3][:], in_=wu[oc]),
                       inc=e_wld[oc % 3], incv=16)
                for ts in range(NT):
                    m = oc * NT + ts
                    for c in range(NCH):
                        p.emit("tensor", lambda t, oc=oc, c=c, ts=ts, m=m: t.matmul(
                                   psG[m % 3][:], wgbuf[oc % 3][:, c*128:(c+1)*128],
                                   xn_all[:, c*SC + ts*T: c*SC + (ts+1)*T],
                                   start=(c == 0), stop=(c == NCH - 1)),
                               waits=[(e_wld[oc % 3], 32 * (oc // 3 + 1)), (e_gev, m - 2)],
                               inc=(e_gmm if c == NCH - 1 else None))
                    for c in range(NCH):
                        p.emit("tensor", lambda t, oc=oc, c=c, ts=ts, m=m: t.matmul(
                                   psU[m % 3][:], wubuf[oc % 3][:, c*128:(c+1)*128],
                                   xn_all[:, c*SC + ts*T: c*SC + (ts+1)*T],
                                   start=(c == 0), stop=(c == NCH - 1)),
                               waits=[(e_fev, m - 1)],
                               inc=(e_mm if c == NCH - 1 else None))
                    p.emit("scalar", lambda sc, oc=oc, m=m: sc.activation(
                               gate_s[m % 2][:], psG[m % 3][:], AF.Silu, scale=sg_t[:, oc:oc+1]),
                           waits=[(e_gmm, m + 1), (e_fev, m - 1)], inc=e_gev)
                    p.emit("vector", lambda v, oc=oc, m=m: v.scalar_tensor_tensor(
                               fstg[m % 4][:], psU[m % 3][:], su_t[:, oc:oc+1], gate_s[m % 2][:],
                               ALU.mult, ALU.mult),
                           waits=[(e_mm, m + 1), (e_gev, m + 1), (e_fst[m % 4], 16 * (m // 4))],
                           inc=e_fev)
                    p.emit("sync", lambda s, oc=oc, ts=ts, m=m: s.dma_start(
                               out=fuse_s[ts][:, oc*T:(oc+1)*T], in_=fstg[m % 4][:]),
                           waits=[(e_fev, m + 1)], inc=e_fst[m % 4], incv=16)
                    if do_c and cj is not None and cph == 3:
                        emit_C_p_chunks(cj, ts, 1)
                    if do_c and cj is not None and cph == 4:
                        emit_C_p_chunks(cj, 8 + ts, 1)
        elif do_c:
            for ts in range(NT):
                emit_C_kloads(ts)
                emit_C_ksq(ts)
                emit_C_norm(ts)
                emit_C_p(ts)
            emit_C_fin1()
            emit_C_fin2()
        cest.close()
        p.marker(c_dve, c_af, c_nmm, c_bmm, c_fin, c_kld, *c_vld, c_af1, c_af2, c_af3)
        p.marker(e_mm, e_gmm, e_gev, e_fev, *e_fst, *e_wld, a_st)

    d_wl = SA("d_wl", 4); d_sc = S("d_sc"); d_qld = SA("d_qld", 3)
    d_mm = S("d_mm"); d_ev = S("d_ev"); d_st = SA("d_st", 2)
    f_fld = SA("f_fld", 2)

    if "D" in phases:
        dst_ = ExitStack()
        dsb = lambda n, sh, dt: dst_.enter_context(nc.sbuf_tensor(n, sh, dt))
        woutS = big[:, NOC * T: 2 * NOC * T]
        qsl = [dsb(f"qsl{i}", [128, NCH * T], BF16) for i in range(3)]
        dstg = [dsb(f"dstg{i}", [128, T], BF16) for i in range(2)]
        psD = [dst_.enter_context(nc.psum_tensor(f"psD{i}", [128, T], F32)) for i in range(4)]

        def emit_qsl(ts):
            for c in range(NCH):
                eng = "scalar" if c % 2 == 0 else "sync"
                p.emit(eng, lambda s, c=c, ts=ts: s.dma_start(
                           out=qsl[ts % 3][:, c*T:(c+1)*T], in_=q_s[c][:, ts*T:(ts+1)*T]),
                       waits=[(d_mm, NCH * (ts - 2))], inc=d_qld[ts % 3], incv=16)

        emit_qsl(0)
        e_done = e_mm.n
        for ec in range(4):
            p.emit("gpsimd" if ec % 2 == 0 else "sync", lambda g, ec=ec: g.dma_start(
                       out=woutS[:, ec*2048:(ec+1)*2048], in_=wout_b[ec]),
                   waits=[(d_wl[ec % 4], 16 * (ec // 4)), (e_mm, e_done), (d_wcast, d_wcast.n)],
                   inc=d_wl[ec % 4], incv=16)
        emit_qsl(1)
        for ec in range(4, NCH):
            p.emit("gpsimd" if ec % 2 == 0 else "sync", lambda g, ec=ec: g.dma_start(
                       out=woutS[:, ec*2048:(ec+1)*2048], in_=wout_b[ec]),
                   waits=[(d_wl[ec % 4], 16 * (ec // 4))], inc=d_wl[ec % 4], incv=16)
        emit_qsl(2)
        for ts in range(NT):
            if ts >= 1 and ts + 2 < NT:
                emit_qsl(ts + 2)
            for c in range(NCH):
                p.emit("vector", lambda v, c=c, ts=ts: v.tensor_scalar_mul(
                           qsl[ts % 3][:, c*T:(c+1)*T], qsl[ts % 3][:, c*T:(c+1)*T],
                           A_full[:, c:c+1]),
                       waits=[(d_qld[ts % 3], 256 * (ts // 3 + 1)), (c_af, 1)],
                       inc=(d_sc if c == NCH - 1 else None))
            for ec in range(NCH):
                m = ts * NCH + ec
                for c in range(NCH):
                    p.emit("tensor", lambda t, ec=ec, c=c, ts=ts, m=m: t.matmul(
                               psD[m % 4][:], woutS[:, ec*2048 + c*128: ec*2048 + (c+1)*128],
                               qsl[ts % 3][:, c*T:(c+1)*T],
                               start=(c == 0), stop=(c == NCH - 1)),
                           waits=[(d_sc, ts + 1), (d_wl[ec % 4], 16 * (ec // 4 + 1)), (d_ev, m - 3)],
                           inc=(d_mm if c == NCH - 1 else None))
                p.emit("scalar", lambda sc, ec=ec, m=m: sc.activation(
                           dstg[m % 2][:], psD[m % 4][:], AF.Identity, bias=bo_t[:, ec:ec+1]),
                       waits=[(d_mm, m + 1), (d_st[m % 2], 16 * (m // 2))], inc=d_ev)
                p.emit("sync", lambda s, ec=ec, ts=ts, m=m: s.dma_start(
                           out=yo_s[ec][:, ts*T:(ts+1)*T], in_=dstg[m % 2][:]),
                       waits=[(d_ev, m + 1)], inc=d_st[m % 2], incv=16)
            if "F" in phases and ts < 1:
                FH0 = NOC * T
                for half in range(2):
                    p.emit("scalar", lambda s, tsj=ts, half=half: s.dma_start(
                               out=big[:, (tsj % 2)*FH0 + half*(FH0//2): (tsj % 2)*FH0 + (half+1)*(FH0//2)],
                               in_=fuse_s[tsj][:, half*(FH0//2):(half+1)*(FH0//2)]),
                           inc=f_fld[ts % 2], incv=16)
        dst_.close()
        p.marker(d_mm, d_ev, *d_st, d_sc, *d_qld)

    f_xld = S("f_xld"); f_wld = SA("f_wld", 8); f_mm = S("f_mm")
    f_yld = SA("f_yld", 2); f_xnld = SA("f_xnld", 2); f_r2 = S("f_r2"); f_r3 = S("f_r3")
    f_tr = S("f_tr"); f_ev2 = S("f_ev2"); f_out = SA("f_out", 4)

    if "F" in phases:
        fst = ExitStack()
        fsb = lambda n, sh, dt: fst.enter_context(nc.sbuf_tensor(n, sh, dt))
        wdbuf = [fsb(f"wdbuf{i}", [128, NOC * 128], BF16) for i in range(2)]
        stage = [fsb(f"stage{i}", [128, D], F32) for i in range(4)]
        yo_b = [fsb(f"yob{i}", [128, T], BF16) for i in range(2)]
        xn_b = [fsb(f"xnb{i}", [128, T], BF16) for i in range(2)]
        r3 = [fsb(f"r3_{i}", [128, T], F32) for i in range(3)]
        psF = [fst.enter_context(nc.psum_tensor(f"psF{i}", [128, T], F32)) for i in range(3)]
        psT = [fst.enter_context(nc.psum_tensor(f"psTF{i}", [128, T], F32)) for i in range(3)]

        def emit_tp_adds(hm):
            for g in range(4):
                p.emit("tensor", lambda t, g=g, hm=hm: t.transpose(
                           psT[hm % 3][:, g*128:(g+1)*128], r3[hm % 3][:, g*128:(g+1)*128],
                           id_f32[:]),
                       waits=[(f_r3, hm + 1), (f_ev2, hm - 2)],
                       inc=(f_tr if g == 3 else None))
            h = hm % NCH
            tsi = hm // NCH
            for g in range(4):
                p.emit("vector", lambda v, g=g, h=h, hm=hm: v.tensor_add(
                           stage[g][:, h*128:(h+1)*128], psT[hm % 3][:, g*128:(g+1)*128],
                           stage[g][:, h*128:(h+1)*128]),
                       waits=[(f_tr, hm + 1), (f_xld, 16 * (tsi * 4 + 4))],
                       inc=(f_ev2 if g == 3 else None))

        FH = NOC * T

        def emit_fuse_load(tsj):
            fbj = (tsj % 2) * FH
            for half in range(2):
                p.emit("scalar", lambda s, tsj=tsj, half=half, fbj=fbj: s.dma_start(
                           out=fuse_ts[:, fbj + half*(FH//2): fbj + (half+1)*(FH//2)],
                           in_=fuse_s[tsj][:, half*(FH//2):(half+1)*(FH//2)]),
                       waits=[(f_mm, NCH * (tsj - 1) if tsj >= 2 else 0)], inc=f_fld[tsj % 2], incv=16)

        if "D" not in phases:
            emit_fuse_load(0)
        emit_fuse_load(1)
        for tsi in range(NT):
            fb = (tsi % 2) * FH
            for h in range(NCH):
                if h == 1:
                    for g in range(4):
                        p.emit("sync", lambda s, tsi=tsi, g=g: s.dma_start(
                                   out=stage[g][:], in_=x[tsi*T + g*128: tsi*T + (g+1)*128, :]),
                               waits=[(f_out[g], 16 * tsi)], inc=f_xld, incv=16)
                hm = tsi * NCH + h
                for q in range(4):
                    p.emit("gpsimd", lambda g, h=h, hm=hm, q=q: g.dma_start(
                               out=wdbuf[hm % 2][:, q*16*128:(q+1)*16*128], in_=wd[h, :, q*16:(q+1)*16]),
                           waits=[(f_mm, hm - 1)], inc=f_wld[(hm % 2) * 4 + q], incv=16)
                p.emit("sync", lambda s, h=h, tsi=tsi, hm=hm: s.dma_start(
                           out=yo_b[hm % 2][:], in_=yo_s[h][:, tsi*T:(tsi+1)*T]),
                       waits=[(f_r2, hm - 1)], inc=f_yld[hm % 2], incv=16)
                p.emit("sync", lambda s, h=h, tsi=tsi, hm=hm: s.dma_start(
                           out=xn_b[hm % 2][:], in_=xn_s[h][:, tsi*T:(tsi+1)*T]),
                       waits=[(f_r3, hm - 1)], inc=f_xnld[hm % 2], incv=16)
                if hm >= 2:
                    emit_tp_adds(hm - 2)
                for oc in range(NOC):
                    p.emit("tensor", lambda t, oc=oc, hm=hm, tsi=tsi, fb=fb: t.matmul(
                               psF[hm % 3][:], wdbuf[hm % 2][:, oc*128:(oc+1)*128],
                               fuse_ts[:, fb + oc*T: fb + (oc+1)*T],
                               start=(oc == 0), stop=(oc == NOC - 1)),
                           waits=[(f_wld[(hm % 2) * 4 + oc // 16], 16 * (hm // 2 + 1)),
                                  (f_fld[tsi % 2], 32 * (tsi // 2 + 1)),
                                  (f_r2, hm - 2)],
                           inc=(f_mm if oc == NOC - 1 else None))
                p.emit("vector", lambda v, h=h, hm=hm: v.scalar_tensor_tensor(
                           r3[hm % 3][:], psF[hm % 3][:], sd_t[:, h:h+1], yo_b[hm % 2][:],
                           ALU.mult, ALU.add),
                       waits=[(f_mm, hm + 1), (f_yld[hm % 2], 16 * (hm // 2 + 1)), (f_tr, hm - 2)],
                       inc=f_r2)
                p.emit("vector", lambda v, hm=hm: v.tensor_add(
                           r3[hm % 3][:], r3[hm % 3][:], xn_b[hm % 2][:]),
                       waits=[(f_xnld[hm % 2], 16 * (hm // 2 + 1)), (f_r2, hm + 1)], inc=f_r3)
            if tsi + 2 < NT:
                emit_fuse_load(tsi + 2)
            for g in range(4):
                p.emit("scalar", lambda s, tsi=tsi, g=g: s.dma_start(
                           out=out[tsi*T + g*128: tsi*T + (g+1)*128, :], in_=stage[g][:]),
                       waits=[(f_ev2, (tsi + 1) * NCH)], inc=f_out[g], incv=16)
        emit_tp_adds(NT * NCH - 2)
        emit_tp_adds(NT * NCH - 1)
        fst.close()

    p.emit("sync", None, waits=[(s_, s_.n) for s_ in all_sems if s_.n > 0])

    with nc.Block() as blk:
        @blk.tensor
        def _(t):
            p.replay(t, "tensor")
        @blk.vector
        def _(v):
            p.replay(v, "vector")
        @blk.scalar
        def _(sc):
            p.replay(sc, "scalar")
        @blk.sync
        def _(s):
            p.replay(s, "sync")
        @blk.gpsimd
        def _(g):
            p.replay(g, "gpsimd")
    gst.close()
    return nc


def _prep_weights(L0_attn_W_in, L0_attn_b_in, L0_attn_W_out, L0_attn_b_out,
                  L0_ffn_w_gate, L0_ffn_w_up, L0_ffn_w_down,
                  L0_ffn_s_gate, L0_ffn_s_up, L0_ffn_s_down):
    perm = np.concatenate([np.arange(0, 1024, 2), np.arange(1, 1024, 2),
                           np.arange(1024, 2048)])
    W_in = np.asarray(L0_attn_W_in, np.float32)
    Win_p = np.concatenate([W_in[0:2048][perm], W_in[2048:4096][perm],
                            W_in[4096:6144][perm]], axis=0)
    win = np.ascontiguousarray(Win_p.reshape(NEC, 128, NCH, 128).transpose(0, 3, 2, 1))
    b_in = np.asarray(L0_attn_b_in, np.float32)
    b_in_p = np.concatenate([b_in[0:2048][perm], b_in[2048:4096][perm],
                             b_in[4096:6144][perm]])
    b_in_t = np.ascontiguousarray(b_in_p.reshape(NEC, 128).T)
    Wt = np.asarray(L0_attn_W_out, np.float32)[:, perm].T
    wout = np.ascontiguousarray(Wt.reshape(NCH, 128, NCH, 128).transpose(2, 1, 0, 3))
    b_out_t = np.ascontiguousarray(np.asarray(L0_attn_b_out, np.float32).reshape(NCH, 128).T)
    wg_ = np.asarray(L0_ffn_w_gate, np.int8)
    wu_ = np.asarray(L0_ffn_w_up, np.int8)
    wd_ = np.asarray(L0_ffn_w_down, np.int8)
    wg = np.ascontiguousarray(wg_.reshape(NCH, 128, NOC, 128).transpose(2, 1, 0, 3))
    wu = np.ascontiguousarray(wu_.reshape(NCH, 128, NOC, 128).transpose(2, 1, 0, 3))
    wd = np.ascontiguousarray(wd_.reshape(NOC, 128, NCH, 128).transpose(2, 1, 0, 3))
    sg = np.ascontiguousarray(np.asarray(L0_ffn_s_gate, np.float32).reshape(NOC, 128).T)
    su = np.ascontiguousarray(np.asarray(L0_ffn_s_up, np.float32).reshape(NOC, 128).T)
    sd = np.ascontiguousarray(np.asarray(L0_ffn_s_down, np.float32).reshape(NCH, 128).T)
    return dict(win=win, b_in=b_in_t, wout=wout, b_out=b_out_t,
                wg=wg, wu=wu, wd=wd, sg=sg, su=su, sd=sd)


def _rope_tables(h):
    pos = (np.float32(h * SC) + np.arange(SC, dtype=np.float32))
    invf = (np.float32(10000.0) ** (-(np.arange(0, 1024, 2, dtype=np.float32)
                                      / np.float32(1024.0))))
    fr = invf[:, None].astype(np.float32) * pos[None, :].astype(np.float32)
    return (np.ascontiguousarray(np.cos(fr).astype(np.float32).reshape(4, 128, SC)),
            np.ascontiguousarray(np.sin(fr).astype(np.float32).reshape(4, 128, SC)))


_CACHE = {}

def kernel(L0_attn_X, L0_attn_W_in, L0_attn_b_in, L0_attn_W_out, L0_attn_b_out,
           L0_ffn_w_gate, L0_ffn_w_up, L0_ffn_w_down,
           L0_ffn_s_gate, L0_ffn_s_up, L0_ffn_s_down):
    X = np.asarray(L0_attn_X, np.float32)
    w = _prep_weights(L0_attn_W_in, L0_attn_b_in, L0_attn_W_out, L0_attn_b_out,
                      L0_ffn_w_gate, L0_ffn_w_up, L0_ffn_w_down,
                      L0_ffn_s_gate, L0_ffn_s_up, L0_ffn_s_down)
    tabs = [_rope_tables(0), _rope_tables(1)]
    in_maps = []
    for c in range(8):
        b, h = c // 2, c % 2
        m = dict(w)
        m["x"] = np.ascontiguousarray(X[b, h*SC:(h+1)*SC, :])
        m["cosb"], m["sinb"] = tabs[h]
        in_maps.append(m)
    if "nc" not in _CACHE:
        _CACHE["nc"] = build_program()
    res = run_bass_kernel_spmd(_CACHE["nc"], in_maps, core_ids=list(range(8)))
    outp = np.empty((4, 8192, D), np.float32)
    for c in range(8):
        b, h = c // 2, c % 2
        outp[b, h*SC:(h+1)*SC, :] = res.results[c]["out"]
    return outp
